# version 01
# baseline (speedup 1.0000x reference)
import numpy as np
import ml_dtypes
from contextlib import ExitStack
import concourse.bass as bass
import concourse.mybir as mybir
from concourse.bass_utils import run_bass_kernel_spmd

F32 = mybir.dt.float32
BF16 = mybir.dt.bfloat16
I32 = mybir.dt.int32
AF = mybir.ActivationFunctionType
ALU = mybir.AluOpType
BF = ml_dtypes.bfloat16

NCORES = 8
D = 1024
KC = 8
POS = 1024
TOK = 2048
NH = 16
WIN = 256
NWIN = 4
EXP_HID = 2816
NHC = 22
SH_HID = 1536
SHC = 12
NE = 8

A_ELEMS = 40960
DBG = False
AW_ELEMS = 12288
TWO_PI = float(2 * np.pi)

ENGS = ("sync", "gpsimd", "tensor", "vector", "scalar")


class Prog:

    def __init__(self, nc, es):
        self.nc = nc
        self.es = es
        self.sem = {}
        self.cnt = {}
        self.ops = []
        self.nblocks = 0
        for s in ("sx", "gw", "pe", "dv", "ac", "po", "st", "hd"):
            self.sem[s] = es.enter_context(nc.semaphore(s))
            self.cnt[s] = 0

    def add(self, eng, fn, waits=(), inc=None, k=1):
        mark = None
        if inc is not None:
            self.cnt[inc] += k
            mark = (inc, self.cnt[inc], "hw" if k >= 16 else eng)
        for w in waits:
            if w is not None:
                assert w[1] <= self.cnt[w[0]], "wait on future mark"
        self.ops.append((eng, fn, tuple(w for w in waits if w is not None), inc, k))
        return mark

    def hard(self, e, inst):
        self.cnt["hd"] += 1
        inst.then_inc(self.sem["hd"], 1)
        e.wait_ge(self.sem["hd"], self.cnt["hd"])
        return inst

    def flush(self):
        if not self.ops:
            return
        ops = self.ops
        self.ops = []
        self.nblocks += 1
        with self.nc.Block() as blk:
            for engname in ENGS:
                eops = [o for o in ops if o[0] == engname]
                if not eops:
                    continue

                def body(e, eops=eops, engname=engname):
                    waited = {}
                    for (_, fn, waits, inc, k) in eops:
                        for (sname, val, peng) in waits:
                            if peng == engname:
                                continue
                            if waited.get(sname, -1) >= val:
                                continue
                            e.wait_ge(self.sem[sname], val)
                            waited[sname] = val
                        inst = fn(e)
                        if inc is not None:
                            inst.then_inc(self.sem[inc], k)
                getattr(blk, engname)(body)


class C:
    pass


def _sb(p, c, name, shape, dt):
    return p.es.enter_context(p.nc.sbuf_tensor(name, shape, dt))


def make_ctx(p, nc):
    c = C()
    c.p = p
    c.x_res = _sb(p, c, "x_res", [128, KC * TOK], F32)
    c.arena = _sb(p, c, "arena", [128, A_ELEMS], BF16)
    c.arena_f = c.arena[:].bitcast(F32)
    c.aw = _sb(p, c, "aw", [128, AW_ELEMS], BF16)
    c.lntmp = _sb(p, c, "lntmp", [128, 2048], BF16)
    c.lnsq = _sb(p, c, "lnsq", [128, 2048], BF16)
    c.lnsq_f = c.lnsq[:].bitcast(F32)
    c.lnsq_i = c.lnsq[:].bitcast(I32)
    c.lnft = _sb(p, c, "lnft", [128, 1024], F32)
    c.lnft_i = c.lnft[:].bitcast(I32)
    c.rows = _sb(p, c, "rows", [1, 3072], F32)

    c.ident_f = _sb(p, c, "ident_f", [128, 128], F32)
    c.ident_b = _sb(p, c, "ident_b", [128, 128], BF16)
    c.ones_col_b = _sb(p, c, "ones_col_b", [128, 1], BF16)
    c.ones_col_f = _sb(p, c, "ones_col_f", [128, 1], F32)
    c.onesrow_f = _sb(p, c, "onesrow_f", [1, 128], F32)
    c.monesrow_f = _sb(p, c, "monesrow_f", [1, 128], F32)
    c.halfpi = _sb(p, c, "halfpi", [128, 1], F32)
    c.masks_b = _sb(p, c, "masks_b", [128, 512], BF16)
    c.pos_f = _sb(p, c, "pos_f", [1, 1024], F32)
    c.pmask_f = _sb(p, c, "pmask_f", [1, 8], F32)
    c.theta_rows = _sb(p, c, "theta_rows", [1, 1024], F32)
    c.gam_bet = _sb(p, c, "gam_bet", [128, 16], F32)
    c.l1f_bias = _sb(p, c, "l1f_bias", [128, 32], F32)
    c.in_c_f = _sb(p, c, "in_c_f", [128, 8], F32)
    c.in_c_b = _sb(p, c, "in_c_b", [128, 8], BF16)
    c.ecols = _sb(p, c, "ecols", [8, 8], F32)
    c.gw_sb = _sb(p, c, "gw_sb", [128, 64], F32)
    c.gnb = _sb(p, c, "gnb", [128, 8], F32)
    c.mrcols = _sb(p, c, "mrcols", [128, 16], F32)
    c.small = _sb(p, c, "small", [128, 64], F32)
    c.combtm = _sb(p, c, "combtm", [128, 64], F32)
    c.vtm_sb = _sb(p, c, "vtm_sb", [128, 128], BF16)
    c.ktm_sb = _sb(p, c, "ktm_sb", [128, 256], BF16)
    c.sloc_b = _sb(p, c, "sloc_b", [128, 64], BF16)
    c.rlrow = _sb(p, c, "rlrow", [1, 512], F32)
    c.ub = _sb(p, c, "ub", [64, 256], BF16)
    c.maskcols = _sb(p, c, "maskcols", [128, 8], F32)
    c.bias0 = _sb(p, c, "bias0", [128, 24], F32)

    c.pf = [p.es.enter_context(nc.psum_tensor(f"pf{i}", [128, 512], F32)) for i in range(6)]
    c.pb = [p.es.enter_context(nc.psum_tensor(f"pb{i}", [128, 1024], BF16)) for i in range(2)]

    c.const_loads = []
    return c


def ar(c, off, n):
    return c.arena[:, off:off + n]


def arf(c, off_bf, n):
    assert off_bf % 2 == 0
    return c.arena_f[:, off_bf // 2: off_bf // 2 + n]


def xv(c, ch, b):
    return c.x_res[:, ch * TOK + b * POS: ch * TOK + (b + 1) * POS]



def emit_prologue(p, c, have_x=True):
    if have_x:
        p.add("sync", lambda e: e.dma_start(
            out=c.x_res[:].rearrange("p (kc t) -> p kc t", kc=KC),
            in_=c.d_x_in.rearrange("(kc p) t -> p kc t", p=128)), inc="sx", k=16)
    for dst, src, cast in c.const_loads:
        eng = "gpsimd" if cast else "sync"
        p.add(eng, lambda e, dst=dst, src=src: e.dma_start(out=dst, in_=src),
              inc=("gw" if cast else "sx"), k=16)
    p.add("vector", lambda e: e.memset(c.halfpi[:], float(np.pi / 2)))
    p.add("vector", lambda e: e.memset(c.ones_col_b[:], 1.0))
    p.add("vector", lambda e: e.memset(c.ones_col_f[:], 1.0))
    p.add("vector", lambda e: e.memset(c.onesrow_f[:], 1.0))
    p.add("vector", lambda e: e.memset(c.monesrow_f[:], -1.0))
    p.add("sync", lambda e: e.nop(), waits=[("sx", p.cnt["sx"], "hw")])
    p.add("gpsimd", lambda e: e.nop(), waits=[("gw", p.cnt["gw"], "hw")])
    p.flush()


def emit_epilogue(p, c):
    m = p.add("sync", lambda e: e.dma_start(
        out=c.d_x_out.rearrange("(kc p) t -> p kc t", p=128),
        in_=c.x_res[:].rearrange("p (kc t) -> p kc t", kc=KC)), inc="sx", k=16)
    p.add("sync", lambda e: e.nop(), waits=[("sx", m[1], "hw")])
    p.flush()



def emit_ln(p, c, src_fn, src_is_bf, apply_fn):
    mmm = {}
    for ch in range(KC):
        slot = (ch % 2) * 1024
        if not src_is_bf:
            p.add("vector", lambda e, ch=ch, slot=slot: e.tensor_copy(
                c.lntmp[:, slot:slot + 1024], src_fn(ch)), waits=[mmm.get(ch - 2)])
            xb = c.lntmp[:, slot:slot + 1024]
        else:
            xb = src_fn(ch)
        msq = p.add("vector", lambda e, xb=xb, slot=slot: e.tensor_mul(
            c.lnsq[:, slot:slot + 1024], xb, xb),
            waits=[mmm.get(ch - 2)] if src_is_bf else (), inc="dv")

        def mmfn(e, ch=ch, slot=slot, xb=xb):
            for tt in range(2):
                e.matmul(c.pf[tt][0:1, :], c.ones_col_b[:], xb[:, tt * 512:(tt + 1) * 512],
                         start=(ch == 0), stop=(ch == KC - 1))
                mm = e.matmul(c.pf[2 + tt][0:1, :], c.ones_col_b[:],
                              c.lnsq[:, slot + tt * 512:slot + (tt + 1) * 512],
                              start=(ch == 0), stop=(ch == KC - 1))
            return mm
        mmm[ch] = p.add("tensor", mmfn, waits=[msq], inc="pe")
    mstat = mmm[KC - 1]

    def rows1(e):
        for tt in range(2):
            e.tensor_scalar(c.rows[0:1, tt * 512:(tt + 1) * 512], c.pf[tt][0:1, :],
                            1.0 / D, None, ALU.mult)
            e.tensor_scalar(c.rows[0:1, 2048 + tt * 512:2048 + (tt + 1) * 512],
                            c.pf[2 + tt][0:1, :], 1.0 / D, None, ALU.mult)
        m = c.rows[0:1, 0:1024]
        s = c.rows[0:1, 2048:3072]
        e.tensor_mul(c.rows[0:1, 1024:2048], m, m)
        e.tensor_sub(s, s, c.rows[0:1, 1024:2048])
        return e.tensor_scalar(s, s, 1e-5, None, ALU.add)
    mr1 = p.add("vector", rows1, waits=[mstat], inc="dv")

    def rows2(e):
        e.activation(c.rows[0:1, 1024:2048], c.rows[0:1, 2048:3072], AF.Ln)
        return e.activation(c.rows[0:1, 1024:2048], c.rows[0:1, 1024:2048], AF.Exp, scale=-0.5)
    mr2 = p.add("scalar", rows2, waits=[mr1], inc="ac")
    p.add("vector", lambda e: e.tensor_mul(c.rows[0:1, 2048:3072], c.rows[0:1, 0:1024],
                                           c.rows[0:1, 1024:2048]), waits=[mr2], inc="dv")
    p.flush()

    def bcast(e):
        for tt in range(2):
            e.matmul(c.pf[tt][:, :], c.onesrow_f[:], c.rows[0:1, 1024 + tt * 512:1024 + (tt + 1) * 512],
                     start=True, stop=True)
            mm = e.matmul(c.pf[2 + tt][:, :], c.onesrow_f[:],
                          c.rows[0:1, 2048 + tt * 512:2048 + (tt + 1) * 512], start=True, stop=True)
        return mm
    mb = p.add("tensor", bcast, inc="pe")
    apply_fn(p, mb)
    p.flush()


def emit_pre_ln(p, c, b, dst_fn, src_fn=None):
    src = src_fn if src_fn is not None else (lambda ch: xv(c, ch, b))

    def apply(p2, mb):
        def fn(e):
            for ch in range(KC):
                for tt in range(2):
                    t1 = c.lnft[:, tt * 512:(tt + 1) * 512]
                    e.tensor_mul(t1, src(ch)[:, tt * 512:(tt + 1) * 512], c.pf[tt][:, :])
                    last = e.tensor_sub(dst_fn(ch)[:, tt * 512:(tt + 1) * 512], t1, c.pf[2 + tt][:, :])
            return last
        p2.add("vector", fn, waits=[mb], inc="dv")
    emit_ln(p, c, src, False, apply)


def emit_post_ln_residual(p, c, b, src_fn, src_is_bf=True, x_old_fn=None):
    def apply(p2, mb):
        def fn(e):
            for ch in range(KC):
                for tt in range(2):
                    t1 = c.lnft[:, tt * 512:(tt + 1) * 512]
                    t2 = c.lnsq_f[:, tt * 512:(tt + 1) * 512]
                    xdst = xv(c, ch, b)[:, tt * 512:(tt + 1) * 512]
                    e.tensor_mul(t1, src_fn(ch)[:, tt * 512:(tt + 1) * 512], c.pf[tt][:, :])
                    e.tensor_sub(t2, t1, c.pf[2 + tt][:, :])
                    xold = xdst if x_old_fn is None else x_old_fn(ch)[:, tt * 512:(tt + 1) * 512]
                    last = e.tensor_add(xdst, xold, t2)
            return last
        p2.add("vector", fn, waits=[mb], inc="dv")
    emit_ln(p, c, src_fn, src_is_bf, apply)



def proj_stream(p, c, w_hbm, kcs, mos, rhs_fn, evac_eng, evac_fn, extra=None,
                wbase=0, n_tt=2):
    tile = kcs * 128
    tile2 = extra[0] * 128 if extra else 0
    wm, wm2, pem, evm = {}, {}, {}, {}
    for mo in range(mos):
        buf = mo % 2
        dst = c.aw[:, wbase + buf * tile: wbase + (buf + 1) * tile]
        wm[mo] = p.add("gpsimd", lambda e, dst=dst, mo=mo: e.dma_start(
            out=dst.rearrange("p (kc n) -> p kc n", kc=kcs),
            in_=w_hbm.rearrange("(kc p) m -> p kc m", p=128)[:, :, mo * 128:(mo + 1) * 128]),
            waits=[pem.get(mo - 2)], inc="gw", k=16)
        if extra:
            kcs2, _, w_hbm2 = extra
            dst2 = c.aw[:, wbase + 2 * tile + buf * tile2: wbase + 2 * tile + (buf + 1) * tile2]
            wm2[mo] = p.add("gpsimd", lambda e, dst2=dst2, mo=mo, kcs2=kcs2, w_hbm2=w_hbm2: e.dma_start(
                out=dst2.rearrange("p (kc n) -> p kc n", kc=kcs2),
                in_=w_hbm2.rearrange("(kc p) m -> p kc m", p=128)[:, :, mo * 128:(mo + 1) * 128]),
                waits=[pem.get(mo - 2)], inc="gw", k=16)
        for tt in range(n_tt):
            idx = mo * n_tt + tt
            pbank = c.pf[4 + idx % 2]

            def mmfn(e, mo=mo, tt=tt, buf=buf, pbank=pbank):
                wt = c.aw[:, wbase + buf * tile: wbase + (buf + 1) * tile]
                for kc in range(kcs):
                    last = (extra is None) and (kc == kcs - 1)
                    mm = e.matmul(pbank[:, :], wt[:, kc * 128:(kc + 1) * 128],
                                  rhs_fn(kc, tt), start=(kc == 0), stop=last)
                if extra:
                    kcs2, rhs_fn2, _ = extra
                    wt2 = c.aw[:, wbase + 2 * tile + buf * tile2: wbase + 2 * tile + (buf + 1) * tile2]
                    for kc in range(kcs2):
                        mm = e.matmul(pbank[:, :], wt2[:, kc * 128:(kc + 1) * 128],
                                      rhs_fn2(kc, tt), start=False, stop=(kc == kcs2 - 1))
                return mm
            mmw = [wm[mo], evm.get(idx - 2)]
            if extra:
                mmw.append(wm2[mo])
            m = p.add("tensor", mmfn, waits=mmw, inc="pe")
            if tt == n_tt - 1:
                pem[mo] = m
            evm[idx] = p.add(evac_eng, (lambda e, mo=mo, tt=tt, pbank=pbank: evac_fn(e, mo, tt, pbank)),
                             waits=[m], inc=("ac" if evac_eng == "scalar" else "dv"))
    p.flush()
    return evm



XN0 = 0
L0_QKV = 8192
L0_Y = 32768


def emit_l0_attn(p, c):
    for b in range(2):
        emit_pre_ln(p, c, b, lambda ch: ar(c, XN0 + ch * 1024, 1024))
        proj_stream(p, c, c.d_wattn, KC, 24,
                    lambda kc, tt: ar(c, XN0 + kc * 1024 + tt * 512, 512),
                    "scalar",
                    lambda e, mo, tt, ps: e.activation(
                        ar(c, L0_QKV + mo * 1024 + tt * 512, 512), ps[:, :], AF.Identity))
        _l0_attention(p, c, b)
        proj_stream(p, c, c.d_wproj, KC, KC,
                    lambda kc, tt: ar(c, L0_Y + kc * 1024 + tt * 512, 512),
                    "scalar",
                    lambda e, mo, tt, ps: e.activation(
                        ar(c, XN0 + mo * 1024 + tt * 512, 512), ps[:, :], AF.Identity))
        emit_post_ln_residual(p, c, b, lambda ch: ar(c, XN0 + ch * 1024, 1024))


def _l0_attention(p, c, b):
    prev = {}
    for w in range(NWIN):
        for h in range(NH):
            qch, qp = h // 2, (h % 2) * 64
            qh = ar(c, L0_QKV + qch * 1024 + w * 256, 256)
            kh = ar(c, L0_QKV + (8 + qch) * 1024 + w * 256, 256)
            vh = ar(c, L0_QKV + (16 + qch) * 1024 + w * 256, 256)

            def s_mm(e, qh=qh, kh=kh, qp=qp):
                for jc in range(2):
                    mm = e.matmul(c.pf[jc][:, 0:256], kh[qp:qp + 64, jc * 128:(jc + 1) * 128],
                                  qh[qp:qp + 64, :], start=True, stop=True)
                return mm
            ms = p.add("tensor", s_mm, waits=[prev.get("exp")], inc="pe")

            def exp_fn(e):
                for jc in range(2):
                    i = e.activation(c.lntmp[:, jc * 512:jc * 512 + 256], c.pf[jc][:, 0:256],
                                     AF.Exp, scale=0.125)
                return i
            mexp = p.add("scalar", exp_fn, waits=[ms, prev.get("u_mm"), prev.get("l_mm")], inc="ac")

            def vt_fn(e, vh=vh, qp=qp):
                idb = c.ident_b[qp:qp + 64, qp:qp + 64]
                for jc in range(2):
                    mm = e.transpose(c.pb[jc][:, 0:64], vh[qp:qp + 64, jc * 128:(jc + 1) * 128], idb)
                return mm
            mvt = p.add("tensor", vt_fn, waits=[prev.get("vtm_evac")], inc="pe")

            def vtev(e):
                for jc in range(2):
                    i = e.tensor_copy(c.vtm_sb[:, jc * 64:(jc + 1) * 64], c.pb[jc][:, 0:64])
                return i
            mvte = p.add("vector", vtev, waits=[mvt, prev.get("u_mm")], inc="dv")

            def l_mm(e):
                for jc in range(2):
                    mm = e.matmul(c.pf[2][0:1, 0:256], c.ones_col_b[:],
                                  c.lntmp[:, jc * 512:jc * 512 + 256],
                                  start=(jc == 0), stop=(jc == 1))
                return mm
            ml = p.add("tensor", l_mm, waits=[mexp, prev.get("recip")], inc="pe")

            def u_mm(e):
                for jc in range(2):
                    mm = e.matmul(c.pf[3][0:64, 0:256], c.vtm_sb[:, jc * 64:(jc + 1) * 64],
                                  c.lntmp[:, jc * 512:jc * 512 + 256],
                                  start=(jc == 0), stop=(jc == 1))
                return mm
            mu = p.add("tensor", u_mm, waits=[mexp, mvte, prev.get("u_evac")], inc="pe")

            mrec = p.add("vector", lambda e: e.reciprocal(c.rlrow[0:1, 0:256], c.pf[2][0:1, 0:256]),
                         waits=[ml, prev.get("lb_mm")], inc="dv")
            mlb = p.add("tensor", lambda e: e.matmul(c.pf[4][0:64, 0:256], c.onesrow_f[0:1, 0:64],
                                                     c.rlrow[0:1, 0:256], start=True, stop=True),
                        waits=[mrec, prev.get("ymul")], inc="pe")
            muev = p.add("scalar", lambda e: e.activation(c.ub[:, 0:256], c.pf[3][0:64, 0:256], AF.Identity),
                         waits=[mu, prev.get("ymul")], inc="ac")
            ydst = ar(c, L0_Y + qch * 1024 + w * 256, 256)[qp:qp + 64, :]
            mym = p.add("vector", lambda e, ydst=ydst: e.tensor_tensor(ydst, c.ub[:, 0:256],
                                                                       c.pf[4][0:64, 0:256], ALU.mult),
                        waits=[mlb, muev], inc="dv")
            prev = {"exp": mexp, "u_mm": mu, "l_mm": ml, "vtm_evac": mvte,
                    "recip": mrec, "lb_mm": mlb, "u_evac": muev, "ymul": mym}
    p.flush()



_PROG_CACHE = {}


def build_program(blocks):
    es = ExitStack()
    nc = bass.Bass("TRN2", target_bir_lowering=False, debug=False)
    p = Prog(nc, es)
    c = make_ctx(p, nc)
    c.d_x_in = nc.dram_tensor("x_in", [D, TOK], F32, kind="ExternalInput").ap()
    c.d_x_out = nc.dram_tensor("x_out", [D, TOK], F32, kind="ExternalOutput").ap()

    def cin(name, shape, dt, dst, cast=False):
        t = nc.dram_tensor("d_" + name, shape, dt, kind="ExternalInput").ap()
        c.const_loads.append((dst, t, cast))
        return t
    cin("ident_f", [128, 128], F32, c.ident_f[:])
    cin("ident_b", [128, 128], BF16, c.ident_b[:])
    if "l0f" in blocks or True:
        cin("ecols", [8, 8], F32, c.ecols[:])
    cin("masks_b", [128, 512], BF16, c.masks_b[:])
    cin("pos_f", [1, 1024], F32, c.pos_f[:])
    cin("pmask_f", [1, 8], F32, c.pmask_f[:])

    if "l0a" in blocks:
        c.d_wattn = nc.dram_tensor("wattn", [D, 3 * D], F32, kind="ExternalInput").ap()
        c.d_wproj = nc.dram_tensor("wproj", [D, D], F32, kind="ExternalInput").ap()
    if "l0f" in blocks:
        c.d_gw = nc.dram_tensor("gwT", [D, 8], F32, kind="ExternalInput").ap()
        if DBG:
            c.d_dbg1 = nc.dram_tensor("dbg1", [8, 1024], F32, kind="ExternalOutput").ap()
            c.d_dbg4 = nc.dram_tensor("dbg4", [128, 64], F32, kind="ExternalOutput").ap()
            c.d_dbg5 = nc.dram_tensor("dbg5", [128, 64], F32, kind="ExternalOutput").ap()
            c.d_dbg6 = nc.dram_tensor("dbg6", [128, 16], F32, kind="ExternalOutput").ap()
            c.d_dbg2 = nc.dram_tensor("dbg2", [128, 8192], F32, kind="ExternalOutput").ap()
            c.d_dbg3 = nc.dram_tensor("dbg3", [128, 8192], BF16, kind="ExternalOutput").ap()
        c.d_w1 = nc.dram_tensor("w1", [NE, D, EXP_HID], F32, kind="ExternalInput").ap()
        c.d_w2 = nc.dram_tensor("w2", [NE, EXP_HID, D], F32, kind="ExternalInput").ap()
        c.d_w3 = nc.dram_tensor("w3", [NE, D, EXP_HID], F32, kind="ExternalInput").ap()
        c.d_sw1 = nc.dram_tensor("sw1", [D, SH_HID], F32, kind="ExternalInput").ap()
        c.d_sw2 = nc.dram_tensor("sw2", [SH_HID, D], F32, kind="ExternalInput").ap()
        c.d_sw3 = nc.dram_tensor("sw3", [D, SH_HID], F32, kind="ExternalInput").ap()
        c.d_xsp = nc.dram_tensor("xspill", [128, KC * POS], F32)
    if "l1a" in blocks:
        c.d_wq = nc.dram_tensor("wq", [D, D], F32, kind="ExternalInput").ap()
        c.d_wk = nc.dram_tensor("wk", [D, D], F32, kind="ExternalInput").ap()
        c.d_wv = nc.dram_tensor("wv", [D, D], F32, kind="ExternalInput").ap()
        c.d_wo = nc.dram_tensor("wo", [D, D], F32, kind="ExternalInput").ap()
        c.d_g1 = nc.dram_tensor("g1", [D, 64], F32, kind="ExternalInput").ap()
        c.d_g2 = nc.dram_tensor("g2", [64, D], F32, kind="ExternalInput").ap()
        cin("theta_rows", [1, 1024], F32, c.theta_rows[:])
        cin("gam_bet", [128, 16], F32, c.gam_bet[:])
        c.d_fsp = nc.dram_tensor("foutspill", [128, 2 * KC * POS], BF16)
        c.d_ccin = nc.dram_tensor("ccin", [128, 2048], F32)
        c.d_ccout = nc.dram_tensor("ccout", [NCORES * 128, 2048], F32)
    if "l1f" in blocks:
        c.d_l1f_w = {}
        for nm in ("wf", "uf", "wi", "ui", "wc", "uc", "wo2", "uo"):
            c.d_l1f_w[nm] = nc.dram_tensor(nm, [D, D], F32, kind="ExternalInput").ap()
        cin("l1f_bias", [128, 32], F32, c.l1f_bias[:])
        cin("in_c_f", [128, 8], F32, c.in_c_f[:])
        c.const_loads.append((c.in_c_b[:], c.const_loads[-1][1], True))
        c.d_xsp2 = nc.dram_tensor("xspill2", [128, 2 * KC * POS], F32)

    with es:
        emit_prologue(p, c)
        for bname in blocks:
            if bname == "l0a":
                emit_l0_attn(p, c)
            elif bname == "l0f":
                emit_l0_ffn(p, c)
            elif bname == "l1a":
                emit_l1_attn(p, c)
            elif bname == "l1f":
                emit_l1_ffn(p, c)
        emit_epilogue(p, c)
    return nc


def _host_prep(inputs):
    h = {}
    h["wattn"] = np.ascontiguousarray(inputs["L0_attn_w_attn"].T)
    h["wproj"] = np.ascontiguousarray(inputs["L0_attn_w_proj"].T)
    h["gwT"] = np.ascontiguousarray(inputs["L0_ffn_gate_w"].T)
    h["w1"] = np.ascontiguousarray(inputs["L0_ffn_w1"])
    h["w2"] = np.ascontiguousarray(inputs["L0_ffn_w2"])
    h["w3"] = np.ascontiguousarray(inputs["L0_ffn_w3"])
    h["sw1"] = np.ascontiguousarray(inputs["L0_ffn_sw1"])
    h["sw2"] = np.ascontiguousarray(inputs["L0_ffn_sw2"])
    h["sw3"] = np.ascontiguousarray(inputs["L0_ffn_sw3"])
    h["wq"] = np.ascontiguousarray(inputs["L1_attn_Wq"])
    h["wk"] = np.ascontiguousarray(inputs["L1_attn_Wk"])
    h["wv"] = np.ascontiguousarray(inputs["L1_attn_Wv"])
    h["wo"] = np.ascontiguousarray(inputs["L1_attn_Wo"])
    h["g1"] = np.ascontiguousarray(inputs["L1_attn_G1"])
    h["g2"] = np.ascontiguousarray(inputs["L1_attn_G2"])
    h["theta_rows"] = np.ascontiguousarray(
        inputs["L1_attn_theta"].reshape(16, 64).reshape(8, 128).reshape(1, 1024))
    h["gam_bet"] = np.ascontiguousarray(np.concatenate(
        [inputs["L1_attn_gamma"].reshape(8, 128).T, inputs["L1_attn_beta"].reshape(8, 128).T],
        axis=1))
    for nm, key in (("wf", "L1_ffn_wf"), ("uf", "L1_ffn_uf"), ("wi", "L1_ffn_wi"),
                    ("ui", "L1_ffn_ui"), ("wc", "L1_ffn_wc"), ("uc", "L1_ffn_uc"),
                    ("wo2", "L1_ffn_wo"), ("uo", "L1_ffn_uo")):
        h[nm] = np.ascontiguousarray(inputs[key].T)
    h["l1f_bias"] = np.ascontiguousarray(np.concatenate(
        [inputs["L1_ffn_" + k].reshape(8, 128).T for k in ("bf", "bi", "bc", "bo")], axis=1))
    h["in_c_f"] = np.ascontiguousarray(inputs["L1_ffn_in_c"].reshape(8, 128).T)
    h["ident_f"] = np.eye(128, dtype=np.float32)
    h["ident_b"] = np.eye(128, dtype=BF)
    h["ecols"] = np.eye(8, dtype=np.float32)
    mask = np.zeros((128, 512), dtype=BF)
    for jc in range(2):
        jj = jc * 128 + np.arange(128)
        ii = np.arange(256)
        mask[:, jc * 256:(jc + 1) * 256] = (jj[:, None] <= ii[None, :]).astype(BF)
    h["masks_b"] = mask
    return h


def _x_shards(x):
    out = []
    for cid in range(NCORES):
        xs = x[:, cid * POS:(cid + 1) * POS, :]
        out.append(np.ascontiguousarray(xs.transpose(2, 0, 1).reshape(D, TOK)))
    return out


def _per_core_consts():
    pos, pmask = [], []
    for cid in range(NCORES):
        pos.append((cid * POS + np.arange(POS, dtype=np.float32)).reshape(1, POS))
        pm = (np.arange(8) < cid).astype(np.float32).reshape(1, 8)
        pmask.append(pm)
    return pos, pmask


_INPUT_KEYS = {
    "l0a": ["wattn", "wproj"],
    "l0f": ["gwT", "w1", "w2", "w3", "sw1", "sw2", "sw3", "ecols"],
    "l1a": ["wq", "wk", "wv", "wo", "g1", "g2", "theta_rows", "gam_bet"],
    "l1f": ["wf", "uf", "wi", "ui", "wc", "uc", "wo2", "uo", "l1f_bias", "in_c_f"],
}
_COMMON_KEYS = ["ident_f", "ident_b", "masks_b", "ecols"]
_CIN_KEYS = {"ident_f", "ident_b", "ecols", "masks_b", "pos_f", "pmask_f",
             "theta_rows", "gam_bet", "l1f_bias", "in_c_f"}


def _run_launch(blocks, h, x_shards, pos, pmask):
    key = tuple(blocks)
    if key not in _PROG_CACHE:
        _PROG_CACHE[key] = build_program(blocks)
    nc = _PROG_CACHE[key]
    keys = set(_COMMON_KEYS)
    for b in blocks:
        keys.update(k for k in _INPUT_KEYS[b] if k in h)
    in_maps = []
    for cid in range(NCORES):
        m = {}
        for k in keys:
            if k in h:
                m[("d_" + k) if k in _CIN_KEYS else k] = h[k]
        m["x_in"] = x_shards[cid]
        m["d_pos_f"] = pos[cid]
        m["d_pmask_f"] = pmask[cid]
        in_maps.append(m)
    res = run_bass_kernel_spmd(nc, in_maps, core_ids=list(range(NCORES)))
    return [r["x_out"] for r in res.results]


def kernel(**inputs):
    h = _host_prep(inputs)
    x = np.asarray(inputs["L0_attn_x"], dtype=np.float32)
    xs = _x_shards(x)
    pos, pmask = _per_core_consts()
    for blocks in (["l0a"], ["l0f"], ["l1a"], ["l1f"]):
        xs = _run_launch(blocks, h, xs, pos, pmask)
    out = np.empty((2, NCORES * POS, D), dtype=np.float32)
    for cid in range(NCORES):
        out[:, cid * POS:(cid + 1) * POS, :] = xs[cid].reshape(D, 2, POS).transpose(1, 2, 0)
    return out



M_XN, M_G, M_SIL, M_H3C, M_CT, M_CSB = 0, 8192, 30720, 31744, 32768, 34816


def emit_l0_ffn(p, c):
    combT = c.arena_f[0:8, M_CT // 2: M_CT // 2 + 1024]
    p.add("sync", lambda e: e.dma_start(
        out=c.gw_sb[:].rearrange("p (kc e) -> p kc e", kc=8),
        in_=c.d_gw.rearrange("(kc p) e -> p kc e", p=128)), inc="sx", k=16)
    p.add("sync", lambda e: e.nop(), waits=[("sx", p.cnt["sx"], "hw")])
    p.flush()
    for b in (0, 1):
        emit_pre_ln(p, c, b, lambda ch: ar(c, M_XN + ch * 1024, 1024))
        _moe_gate(p, c, b, combT)
        msp = p.add("sync", lambda e, b=b: e.dma_start(
            out=c.d_xsp.ap().rearrange("p (ch t) -> p ch t", ch=KC),
            in_=c.x_res[:].rearrange("p (ch two t) -> p ch two t", ch=KC, two=2)[:, :, b, :]),
            inc="sx", k=16)
        p.add("sync", lambda e: e.nop(), waits=[msp])
        p.flush()
        for e_i in range(NE + 1):
            if e_i < NE:
                _moe_expert(p, c, b, combT, e_i,
                            c.d_w1[e_i], c.d_w3[e_i], c.d_w2[e_i], NHC, True, e_i == 0)
            else:
                _moe_expert(p, c, b, combT, 0, c.d_sw1, c.d_sw3, c.d_sw2, SHC, False, False)
        if DBG and b == 0:
            md1 = p.add("sync", lambda e: e.dma_start(out=c.d_dbg1, in_=combT), inc="sx", k=16)
            p.add("sync", lambda e: e.dma_start(out=c.d_dbg4, in_=c.combtm[:]), inc="sx", k=16)
            p.add("sync", lambda e: e.dma_start(out=c.d_dbg5, in_=c.small[:]), inc="sx", k=16)
            p.add("sync", lambda e: e.dma_start(out=c.d_dbg6, in_=c.mrcols[:]), inc="sx", k=16)
            md2 = p.add("sync", lambda e: e.dma_start(
                out=c.d_dbg2.rearrange("p (ch t) -> p ch t", ch=KC),
                in_=c.x_res[:].rearrange("p (ch two t) -> p ch two t", ch=KC, two=2)[:, :, 0, :]),
                inc="sx", k=16)
            md3 = p.add("sync", lambda e: e.dma_start(
                out=c.d_dbg3.rearrange("p (ch t) -> p ch t", ch=KC),
                in_=c.arena[:, M_XN:M_XN + 8192].rearrange("p (ch t) -> p ch t", ch=KC)),
                inc="sx", k=16)
            p.add("sync", lambda e: e.nop(), waits=[md3])
            p.flush()
        mld = p.add("sync", lambda e: e.dma_start(
            out=arf(c, M_G, 8192).rearrange("p (ch t) -> p ch t", ch=KC),
            in_=c.d_xsp.ap().rearrange("p (ch t) -> p ch t", ch=KC)), inc="sx", k=16)
        p.add("sync", lambda e: e.nop(), waits=[mld])
        p.flush()
        emit_post_ln_residual(p, c, b, lambda ch: xv(c, ch, b), src_is_bf=False,
                              x_old_fn=lambda ch: arf(c, M_G + ch * 2048, 1024))


def _moe_gate(p, c, b, combT):
    sm = c.small

    def gsum_mm(e):
        for kc in range(KC):
            mm = e.matmul(c.pf[0][0:1, 0:8], c.ones_col_f[:], c.gw_sb[:, kc * 8:(kc + 1) * 8],
                          start=(kc == 0), stop=(kc == KC - 1))
        return mm
    m1 = p.add("tensor", gsum_mm, inc="pe")
    m2 = p.add("vector", lambda e: e.tensor_copy(sm[0:1, 0:8], c.pf[0][0:1, 0:8]),
               waits=[m1], inc="dv")
    m3 = p.add("tensor", lambda e: e.matmul(c.pf[1][:, 0:8], c.monesrow_f[:], sm[0:1, 0:8],
                                            start=True, stop=True), waits=[m2], inc="pe")
    m4 = p.add("vector", lambda e: e.tensor_copy(c.gnb[:], c.pf[1][:, 0:8]), waits=[m3], inc="dv")
    prev = {}
    for tth in range(8):
        def tr_mr(e, tth=tth):
            e.transpose(c.pf[2][:, 0:1], c.rows[0:1, tth * 128:(tth + 1) * 128],
                        c.ident_f[0:1, 0:1])
            return e.transpose(c.pf[3][:, 0:1], c.rows[0:1, 1024 + tth * 128:1024 + (tth + 1) * 128],
                               c.ident_f[0:1, 0:1])
        mt = p.add("tensor", tr_mr, waits=[prev.get("mrev")], inc="pe")
        mev = p.add("vector", lambda e: (e.tensor_copy(c.mrcols[:, 0:1], c.pf[2][:, 0:1]),
                                         e.tensor_copy(c.mrcols[:, 1:2], c.pf[3][:, 0:1]))[-1],
                    waits=[mt, prev.get("sm1")], inc="dv")

        def lg_mm(e, tth=tth):
            for kc in range(KC):
                mm = e.matmul(c.pf[4][:, 0:8], xv(c, kc, b)[:, tth * 128:(tth + 1) * 128],
                              c.gw_sb[:, kc * 8:(kc + 1) * 8],
                              start=(kc == 0), stop=(kc == KC - 1))
            return mm
        mlg = p.add("tensor", lg_mm, waits=[prev.get("sm1")], inc="pe")

        def softmax_fn(e):
            lg = sm[:, 16:24]
            e.scalar_tensor_tensor(lg, c.gnb[:], c.mrcols[:, 0:1], c.pf[4][:, 0:8],
                                   ALU.mult, ALU.add)
            return e.tensor_scalar(lg, lg, c.mrcols[:, 1:2], None, ALU.mult)
        mdv1 = p.add("vector", softmax_fn, waits=[mev, mlg, m4, prev.get("exp")], inc="dv")
        mexp = p.add("scalar", lambda e: e.activation(sm[:, 26:34], sm[:, 16:24], AF.Exp),
                     waits=[mdv1], inc="ac")

        def top2_fn(e, tth=tth):
            el = sm[:, 26:34]
            p.hard(e, e.reduce_sum(sm[:, 34:35], el, axis=mybir.AxisListType.X))
            p.hard(e, e.reciprocal(sm[:, 35:36], sm[:, 34:35]))
            sc = sm[:, 36:44]
            p.hard(e, e.tensor_tensor(sc, el, sm[:, 35:36].to_broadcast([128, 8]), ALU.mult))
            p.hard(e, e.reduce_max(sm[:, 24:25], sc, axis=mybir.AxisListType.X))
            msk = sm[:, 26:34]
            p.hard(e, e.tensor_tensor(msk, sc, sm[:, 24:25].to_broadcast([128, 8]), ALU.is_ge))
            p.hard(e, e.tensor_sub(msk, sc, msk))
            p.hard(e, e.reduce_max(sm[:, 25:26], msk, axis=mybir.AxisListType.X))
            p.hard(e, e.tensor_tensor(msk, sc, sm[:, 25:26].to_broadcast([128, 8]), ALU.is_ge))
            if DBG and tth == 7:
                e.tensor_copy(sm[:, 48:56], sc)
                e.tensor_copy(sm[:, 56:57], sm[:, 34:35])
                e.tensor_copy(sm[:, 57:58], sm[:, 35:36])
                e.tensor_copy(sm[:, 58:59], sm[:, 24:25])
                e.tensor_copy(sm[:, 59:60], sm[:, 25:26])
            return e.tensor_mul(c.combtm[:, tth * 8:(tth + 1) * 8], sc, msk)
        mtp = p.add("vector", top2_fn, waits=[mexp], inc="dv")
        mtr = p.add("tensor", lambda e, tth=tth: e.transpose(
            c.pf[5][0:8, 0:128], c.combtm[:, tth * 8:(tth + 1) * 8], c.ident_f[:]),
            waits=[mtp, prev.get("ctev")], inc="pe")
        mce = p.add("vector", lambda e, tth=tth: e.tensor_copy(
            combT[:, tth * 128:(tth + 1) * 128], c.pf[5][0:8, 0:128]), waits=[mtr], inc="dv")
        prev = {"mrev": mev, "sm1": mdv1, "exp": mexp, "ctev": mce}
    p.flush()


def _moe_expert(p, c, b, combT, ei, w1ap, w3ap, w2ap, nh, use_comb, first):
    prev = {}
    csb = arf(c, M_CSB, 1024)
    if use_comb:
        mx4 = None
        for tt in range(2):
            mx1 = p.add("tensor", lambda e, tt=tt: e.matmul(
                c.pf[0][0:1, :], c.ecols[:, ei:ei + 1],
                combT[:, tt * 512:(tt + 1) * 512], start=True, stop=True), inc="pe")
            mx2 = p.add("vector", lambda e: e.tensor_copy(c.rlrow[0:1, 0:512], c.pf[0][0:1, :]),
                        waits=[mx1], inc="dv")
            mx3 = p.add("tensor", lambda e: e.matmul(c.pf[1][:, :], c.onesrow_f[:],
                                                     c.rlrow[0:1, 0:512], start=True, stop=True),
                        waits=[mx2], inc="pe")
            mx4 = p.add("vector", lambda e, tt=tt: e.tensor_copy(
                csb[:, tt * 512:(tt + 1) * 512], c.pf[1][:, :]),
                waits=[mx3], inc="dv")
        prev["csb"] = mx4

    wm = {}
    for hg in range(nh):
        buf = hg % 2
        wm[hg] = p.add("gpsimd", lambda e, hg=hg, buf=buf: e.dma_start(
            out=c.aw[:, buf * 1024:(buf + 1) * 1024].rearrange("p (kc n) -> p kc n", kc=KC),
            in_=w1ap.rearrange("(kc p) h -> p kc h", p=128)[:, :, hg * 128:(hg + 1) * 128]),
            waits=[prev.get(("h1", hg - 2))], inc="gw", k=16)
        wm[(hg, 3)] = p.add("gpsimd", lambda e, hg=hg, buf=buf: e.dma_start(
            out=c.aw[:, 2048 + buf * 1024:2048 + (buf + 1) * 1024].rearrange("p (kc n) -> p kc n", kc=KC),
            in_=w3ap.rearrange("(kc p) h -> p kc h", p=128)[:, :, hg * 128:(hg + 1) * 128]),
            waits=[prev.get(("h3", hg - 2))], inc="gw", k=16)
        for tt in range(2):
            idx = hg * 2 + tt
            bh1 = c.pf[0 + idx % 2]
            bh3 = c.pf[2 + idx % 2]

            def h1mm(e, hg=hg, tt=tt, buf=buf, bh1=bh1):
                for kc in range(KC):
                    mm = e.matmul(bh1[:, :], c.aw[:, buf * 1024 + kc * 128: buf * 1024 + (kc + 1) * 128],
                                  ar(c, M_XN + kc * 1024 + tt * 512, 512),
                                  start=(kc == 0), stop=(kc == KC - 1))
                return mm
            mh1 = p.add("tensor", h1mm, waits=[wm[hg], prev.get(("sil", idx - 2)),
                                               prev.get("csb")], inc="pe")

            def h3mm(e, hg=hg, tt=tt, buf=buf, bh3=bh3):
                for kc in range(KC):
                    mm = e.matmul(bh3[:, :],
                                  c.aw[:, 2048 + buf * 1024 + kc * 128: 2048 + buf * 1024 + (kc + 1) * 128],
                                  ar(c, M_XN + kc * 1024 + tt * 512, 512),
                                  start=(kc == 0), stop=(kc == KC - 1))
                return mm
            mh3 = p.add("tensor", h3mm, waits=[wm[(hg, 3)], prev.get(("h3c", idx - 2))], inc="pe")

            msil = p.add("scalar", lambda e, tt=tt, bh1=bh1: e.activation(
                ar(c, M_SIL + tt * 512, 512), bh1[:, :], AF.Silu),
                waits=[mh1, prev.get(("g", idx - 2))], inc="ac")
            prev[("sil", idx)] = msil
            if use_comb:
                mh3c = p.add("vector", lambda e, tt=tt, bh3=bh3: e.tensor_tensor(
                    ar(c, M_H3C + tt * 512, 512), bh3[:, :],
                    csb[:, tt * 512:(tt + 1) * 512], ALU.mult),
                    waits=[mh3, prev.get("csb"), prev.get(("g", idx - 2))], inc="dv")
            else:
                mh3c = p.add("vector", lambda e, tt=tt, bh3=bh3: e.tensor_copy(
                    ar(c, M_H3C + tt * 512, 512), bh3[:, :]),
                    waits=[mh3, prev.get(("g", idx - 2))], inc="dv")
            prev[("h3c", idx)] = mh3c
            mg = p.add("vector", lambda e, hg=hg, tt=tt: e.tensor_mul(
                ar(c, M_G + hg * 1024 + tt * 512, 512),
                ar(c, M_SIL + tt * 512, 512), ar(c, M_H3C + tt * 512, 512)),
                waits=[msil, mh3c], inc="dv")
            prev[("g", idx)] = mg
            if tt == 1:
                prev[("h1", hg)] = mh1
                prev[("h3", hg)] = mh3
    glast = prev[("g", nh * 2 - 1)]

    for mo in range(KC):
        buf = mo % 2
        w2m = p.add("gpsimd", lambda e, mo=mo, buf=buf: e.dma_start(
            out=c.aw[:, 4096 + buf * nh * 128: 4096 + (buf + 1) * nh * 128].rearrange(
                "p (hc n) -> p hc n", hc=nh),
            in_=w2ap.rearrange("(hc p) m -> p hc m", p=128)[:, :, mo * 128:(mo + 1) * 128]),
            waits=[prev.get(("ymm", mo - 2))], inc="gw", k=16)
        for tt in range(2):
            idx = mo * 2 + tt
            pbank = c.pf[4 + idx % 2]

            def ymm(e, mo=mo, tt=tt, buf=buf, pbank=pbank):
                for hc in range(nh):
                    mm = e.matmul(pbank[:, :],
                                  c.aw[:, 4096 + buf * nh * 128 + hc * 128: 4096 + buf * nh * 128 + (hc + 1) * 128],
                                  ar(c, M_G + hc * 1024 + tt * 512, 512),
                                  start=(hc == 0), stop=(hc == nh - 1))
                return mm
            my = p.add("tensor", ymm, waits=[w2m, glast, prev.get(("yev", idx - 2))], inc="pe")
            if tt == 1:
                prev[("ymm", mo)] = my
            yv = xv(c, mo, b)[:, tt * 512:(tt + 1) * 512]
            if first:
                mye = p.add("vector", lambda e, yv=yv, pbank=pbank: e.tensor_copy(yv, pbank[:, :]),
                            waits=[my], inc="dv")
            else:
                mye = p.add("vector", lambda e, yv=yv, pbank=pbank: e.tensor_add(yv, yv, pbank[:, :]),
                            waits=[my], inc="dv")
            prev[("yev", idx)] = mye
    p.flush()




M2_XNO, M2_QF, M2_K, M2_V = 0, 8192, 16384, 24576
M2_COS, M2_SIN, M2_QR, M2_KR, M2_ST = 32768, 33792, 34816, 35840, 36864
P2_XN, P2_Q, P2_OE, P2_FOUT = 0, 8192, 16384, 24576
P2_SEXT, P2_COS, P2_SIN, P2_QR, P2_G1 = 32768, 34816, 35840, 36864, 37888


def emit_l1_attn(p, c):
    m1 = p.add("tensor", lambda e: e.matmul(c.pf[0][:, 0:8], c.onesrow_f[:], c.pmask_f[:],
                                            start=True, stop=True), inc="pe")
    p.add("vector", lambda e: e.tensor_copy(c.maskcols[:], c.pf[0][:, 0:8]), waits=[m1], inc="dv")
    p.flush()

    statesf = arf(c, M2_ST, 2048)
    for b in (0, 1):
        emit_pre_ln(p, c, b, lambda ch: ar(c, M2_XNO + ch * 1024, 1024))
        proj_stream(p, c, c.d_wk, KC, KC,
                    lambda kc, tt: ar(c, M2_XNO + kc * 1024 + tt * 512, 512),
                    "scalar", lambda e, mo, tt, ps: e.activation(
                        ar(c, M2_K + mo * 1024 + tt * 512, 512), ps[:, :], AF.Silu))
        proj_stream(p, c, c.d_wv, KC, KC,
                    lambda kc, tt: ar(c, M2_XNO + kc * 1024 + tt * 512, 512),
                    "scalar", lambda e, mo, tt, ps: e.activation(
                        ar(c, M2_V + mo * 1024 + tt * 512, 512), ps[:, :], AF.Identity))
        proj_stream(p, c, c.d_wq, KC, KC,
                    lambda kc, tt: ar(c, M2_XNO + kc * 1024 + tt * 512, 512),
                    "scalar", lambda e, mo, tt, ps: e.activation(
                        ar(c, M2_QF + mo * 1024 + tt * 512, 512), ps[:, :], AF.Silu))
        for h in range(NH):
            if h % 2 == 0:
                _l1a_rotary(p, c, h // 2, M2_COS, M2_SIN)
            _l1a_build_rot(p, c, h, M2_QF, M2_QR, M2_COS, M2_SIN, True)
            _l1a_build_rot(p, c, h, M2_K, M2_KR, M2_COS, M2_SIN, False)
            _l1a_head(p, c, b, h)
        p.flush()
        proj_stream(p, c, c.d_wo, KC, KC,
                    lambda kc, tt: ar(c, M2_XNO + kc * 1024 + tt * 512, 512),
                    "scalar", lambda e, mo, tt, ps: e.activation(
                        ar(c, M2_QF + mo * 1024 + tt * 512, 512), ps[:, :], AF.Identity))
        msp = p.add("sync", lambda e, b=b: e.dma_start(
            out=c.d_fsp.ap()[:, b * 8192:(b + 1) * 8192], in_=ar(c, M2_QF, 8192)),
            inc="sx", k=16)
        p.add("sync", lambda e: e.nop(), waits=[msp])
        p.flush()

    mst = p.add("gpsimd", lambda e: e.dma_start(out=c.d_ccin.ap(), in_=statesf),
                inc="gw", k=16)
    mcc = p.add("gpsimd", lambda e: e.collective_compute(
        "AllGather", ALU.bypass, replica_groups=[list(range(NCORES))],
        ins=[c.d_ccin.ap().opt()], outs=[c.d_ccout.ap().opt()]),
        waits=[mst], inc="po", k=1)
    rankbuf = arf(c, M2_COS, 2048)
    prevdv = None
    for r in range(NCORES):
        mld = p.add("sync", lambda e, r=r: e.dma_start(
            out=rankbuf, in_=c.d_ccout.ap()[r * 128:(r + 1) * 128, :]),
            waits=[mcc, prevdv], inc="sx", k=16)
        if r == 0:
            prevdv = p.add("vector", lambda e: e.tensor_scalar(
                statesf, rankbuf, c.maskcols[:, 0:1], None, ALU.mult),
                waits=[mld], inc="dv")
        else:
            prevdv = p.add("vector", lambda e, r=r: e.scalar_tensor_tensor(
                statesf, rankbuf, c.maskcols[:, r:r + 1], statesf, ALU.mult, ALU.add),
                waits=[mld], inc="dv")
    p.add("vector", lambda e: e.tensor_copy(ar(c, P2_SEXT, 2048), statesf),
          waits=[prevdv], inc="dv")
    p.flush()

    for b in (0, 1):
        emit_pre_ln(p, c, b, lambda ch: ar(c, P2_XN + ch * 1024, 1024))
        proj_stream(p, c, c.d_wq, KC, KC,
                    lambda kc, tt: ar(c, P2_XN + kc * 1024 + tt * 512, 512),
                    "scalar", lambda e, mo, tt, ps: e.activation(
                        ar(c, P2_Q + mo * 1024 + tt * 512, 512), ps[:, :], AF.Silu))
        mg1w = p.add("gpsimd", lambda e: e.dma_start(
            out=c.aw[:, 0:512].rearrange("p (kc n) -> p kc n", kc=KC),
            in_=c.d_g1.rearrange("(kc p) m -> p kc m", p=128)), inc="gw", k=16)
        mg1 = {}
        for tt in range(2):
            def g1mm(e, tt=tt):
                for kc in range(KC):
                    mm = e.matmul(c.pf[tt][0:64, :], c.aw[:, kc * 64:(kc + 1) * 64],
                                  ar(c, P2_XN + kc * 1024 + tt * 512, 512),
                                  start=(kc == 0), stop=(kc == KC - 1))
                return mm
            mm1 = p.add("tensor", g1mm, waits=[mg1w], inc="pe")
            mg1[tt] = p.add("scalar", lambda e, tt=tt: e.activation(
                ar(c, P2_G1, 1024)[0:64, tt * 512:(tt + 1) * 512], c.pf[tt][0:64, :],
                AF.Identity), waits=[mm1], inc="ac")
        mfl = p.add("sync", lambda e, b=b: e.dma_start(
            out=ar(c, P2_FOUT, 8192), in_=c.d_fsp.ap()[:, b * 8192:(b + 1) * 8192]),
            inc="sx", k=16)
        p.add("sync", lambda e: e.nop(), waits=[mfl])
        p.flush()
        prev = {}
        for h in range(NH):
            if h % 2 == 0:
                _l1a_rotary(p, c, h // 2, P2_COS, P2_SIN)
            _l1a_build_rot(p, c, h, P2_Q, P2_QR, P2_COS, P2_SIN, True)
            qch, qp = h // 2, (h % 2) * 64
            sidx = b * 16 + h
            for tt in range(2):
                pbank = c.pf[2 + tt]

                def oemm(e, sidx=sidx, tt=tt, pbank=pbank):
                    return e.matmul(pbank[0:64, :],
                                    ar(c, P2_SEXT + sidx * 64, 64),
                                    ar(c, P2_QR + tt * 512, 512), start=True, stop=True)
                mo1 = p.add("tensor", oemm, waits=[prev.get(("oev", h - 1, tt))], inc="pe")
                moe = p.add("scalar", lambda e, qch=qch, qp=qp, tt=tt, pbank=pbank: e.activation(
                    ar(c, P2_OE + qch * 1024 + tt * 512, 512)[qp:qp + 64, :],
                    pbank[0:64, :], AF.Identity), waits=[mo1], inc="ac")
                prev[("oev", h, tt)] = moe
            p.flush()
        p.flush()
        proj_stream(p, c, c.d_wo, KC, KC,
                    lambda kc, tt: ar(c, P2_OE + kc * 1024 + tt * 512, 512),
                    "vector", lambda e, mo, tt, ps: e.tensor_add(
                        ar(c, P2_FOUT + mo * 1024 + tt * 512, 512),
                        ar(c, P2_FOUT + mo * 1024 + tt * 512, 512), ps[:, :]))
        mg2w = p.add("gpsimd", lambda e: e.dma_start(
            out=c.aw[0:64, 0:1024], in_=c.d_g2), inc="gw", k=16)
        prevg = {}
        for mo in range(KC):
            for tt in range(2):
                idx = mo * 2 + tt
                pbank = c.pf[4 + idx % 2]
                mmk = p.add("tensor", lambda e, mo=mo, tt=tt, pbank=pbank: e.matmul(
                    pbank[:, :], c.aw[0:64, mo * 128:(mo + 1) * 128],
                    ar(c, P2_G1, 1024)[0:64, tt * 512:(tt + 1) * 512],
                    start=True, stop=True),
                    waits=[mg2w, mg1[tt], prevg.get(idx - 2)], inc="pe")
                mge = p.add("scalar", lambda e, mo=mo, tt=tt, pbank=pbank: e.activation(
                    ar(c, P2_OE + mo * 1024 + tt * 512, 512), pbank[:, :], AF.Sigmoid),
                    waits=[mmk], inc="ac")
                prevg[idx] = mge
        p.flush()

        def gmul(e):
            for ch in range(KC):
                last = e.tensor_mul(ar(c, P2_FOUT + ch * 1024, 1024),
                                    ar(c, P2_FOUT + ch * 1024, 1024),
                                    ar(c, P2_OE + ch * 1024, 1024))
            return last
        p.add("vector", gmul, inc="dv")
        p.flush()
        _l1a_affine_ln(p, c, b)
        emit_post_ln_residual(p, c, b, lambda ch: ar(c, P2_FOUT + ch * 1024, 1024))


def _l1a_affine_ln(p, c, b):
    def apply(p2, mb):
        def fn(e):
            for ch in range(KC):
                for tt in range(2):
                    t1 = c.lnft[:, tt * 512:(tt + 1) * 512]
                    t2 = c.lnsq_f[:, tt * 512:(tt + 1) * 512]
                    src = ar(c, P2_FOUT + ch * 1024 + tt * 512, 512)
                    e.tensor_mul(t1, src, c.pf[tt][:, :])
                    e.tensor_sub(t2, t1, c.pf[2 + tt][:, :])
                    last = e.tensor_scalar(src, t2, c.gam_bet[:, ch:ch + 1],
                                           c.gam_bet[:, 8 + ch:8 + ch + 1],
                                           ALU.mult, ALU.add)
            return last
        p2.add("vector", fn, waits=[mb], inc="dv")
    emit_ln(p, c, lambda ch: ar(c, P2_FOUT + ch * 1024, 1024), True, apply)


def _l1a_rotary(p, c, pr, cos_off, sin_off):
    for half in range(2):
        mang = p.add("tensor", lambda e, pr=pr, half=half: e.matmul(
            c.pf[3 + half][:, :], c.theta_rows[0:1, pr * 128:(pr + 1) * 128],
            c.pos_f[0:1, half * 512:(half + 1) * 512], start=True, stop=True), inc="pe")

        def redfn(e, half=half):
            kf = c.lnft[:, half * 512:(half + 1) * 512]
            ki = c.lnsq_i[:, half * 512:(half + 1) * 512]
            angr = c.lnsq_f[:, half * 512:(half + 1) * 512]
            pa = c.pf[3 + half][:, :]
            p.hard(e, e.tensor_scalar(kf, pa, 1.0 / TWO_PI, None, ALU.mult))
            p.hard(e, e.tensor_copy(ki, kf))
            p.hard(e, e.tensor_copy(kf, ki))
            return e.scalar_tensor_tensor(angr, kf, -TWO_PI, pa, ALU.mult, ALU.add)
        mrd = p.add("vector", redfn, waits=[mang], inc="dv")
        mcs = p.add("scalar", lambda e, half=half: (
            e.activation(ar(c, cos_off + half * 512, 512),
                         c.lnsq_f[:, half * 512:(half + 1) * 512], AF.Sin, bias=c.halfpi[:]),
            e.activation(ar(c, sin_off + half * 512, 512),
                         c.lnsq_f[:, half * 512:(half + 1) * 512], AF.Sin))[-1],
            waits=[mrd], inc="ac")
        p.add("tensor", lambda e: e.nop(), waits=[mcs])
    p.flush()


def _l1a_build_rot(p, c, h, src_off, dst_off, cos_off, sin_off, is_q):
    qch, qp = h // 2, (h % 2) * 64

    def fn(e):
        src = ar(c, src_off + qch * 1024, 1024)[qp:qp + 64, :]
        e.tensor_mul(ar(c, dst_off, 1024)[0:64, :], src,
                     ar(c, cos_off, 1024)[qp:qp + 64, :])
        return e.tensor_mul(ar(c, dst_off, 1024)[64:128, :], src,
                            ar(c, sin_off, 1024)[qp:qp + 64, :])
    p.add("vector", fn, inc="dv")
    p.flush()


def _l1a_head(p, c, b, h):
    qch, qp = h // 2, (h % 2) * 64
    sidx = b * 16 + h
    prev = {}
    for ic in range(4):
        co = ic * 256
        mkt = p.add("tensor", lambda e, co=co: (
            e.transpose(c.pb[0][:, 0:128], ar(c, M2_KR + co, 128), c.ident_b[:]),
            e.transpose(c.pb[1][:, 0:128], ar(c, M2_KR + co + 128, 128), c.ident_b[:]))[-1],
            waits=[prev.get("vte")], inc="pe")
        mke = p.add("vector", lambda e: (
            e.tensor_copy(c.ktm_sb[:, 0:128], c.pb[0][:, 0:128]),
            e.tensor_copy(c.ktm_sb[:, 128:256], c.pb[1][:, 0:128]))[-1],
            waits=[mkt, prev.get("smm")], inc="dv")
        mvt = p.add("tensor", lambda e, co=co: (
            e.transpose(c.pb[0][:, 0:64], ar(c, M2_V + qch * 1024 + co, 128)[qp:qp + 64, :],
                        c.ident_b[qp:qp + 64, qp:qp + 64]),
            e.transpose(c.pb[1][:, 0:64], ar(c, M2_V + qch * 1024 + co + 128, 128)[qp:qp + 64, :],
                        c.ident_b[qp:qp + 64, qp:qp + 64]))[-1],
            waits=[mke], inc="pe")
        mve = p.add("vector", lambda e: (
            e.tensor_copy(c.vtm_sb[:, 0:64], c.pb[0][:, 0:64]),
            e.tensor_copy(c.vtm_sb[:, 64:128], c.pb[1][:, 0:64]))[-1],
            waits=[mvt, prev.get("smm"), prev.get("omm")], inc="dv")
        prev["vte"] = mve
        mst = p.add("tensor", lambda e, co=co: (
            e.matmul(c.pf[0][:, 0:256], ar(c, M2_KR + co, 128), ar(c, M2_QR + co, 256),
                     start=True, stop=True),
            e.matmul(c.pf[1][:, 0:256], ar(c, M2_KR + co + 128, 128), ar(c, M2_QR + co, 256),
                     start=True, stop=True))[-1],
            waits=[prev.get("pt")], inc="pe")
        mpt = p.add("vector", lambda e: (
            e.tensor_tensor(c.lntmp[:, 0:256], c.pf[0][:, 0:256], c.masks_b[:, 0:256], ALU.mult),
            e.tensor_tensor(c.lntmp[:, 512:768], c.pf[1][:, 0:256], c.masks_b[:, 256:512], ALU.mult))[-1],
            waits=[mst, prev.get("omm")], inc="dv")
        prev["pt"] = mpt
        def ommfn(e, ic=ic):
            e.matmul(c.pf[2][0:64, 0:256], c.vtm_sb[:, 0:64], c.lntmp[:, 0:256],
                     start=True, stop=False)
            mm = e.matmul(c.pf[2][0:64, 0:256], c.vtm_sb[:, 64:128], c.lntmp[:, 512:768],
                          start=False, stop=(ic == 0))
            if ic > 0:
                mm = e.matmul(c.pf[2][0:64, 0:256], c.sloc_b[:, 0:64],
                              ar(c, M2_QR + ic * 256, 256), start=False, stop=True)
            return mm
        momm = p.add("tensor", ommfn, waits=[mpt, mve, prev.get("oev"),
                                             prev.get("sle")], inc="pe")
        prev["omm"] = momm
        msmm = p.add("tensor", lambda e, ic=ic: (
            e.matmul(c.pf[5][:, 0:64], c.ktm_sb[:, 0:128], c.vtm_sb[:, 0:64],
                     start=(ic == 0), stop=False),
            e.matmul(c.pf[5][:, 0:64], c.ktm_sb[:, 128:256], c.vtm_sb[:, 64:128],
                     start=False, stop=(ic == 3)))[-1],
            waits=[mke, mve, prev.get("sle"), prev.get("ste")], inc="pe")
        prev["smm"] = msmm
        if ic < 3:
            msl = p.add("vector", lambda e: e.tensor_copy(c.sloc_b[:, 0:64], c.pf[5][:, 0:64]),
                        waits=[msmm, prev.get("omm")], inc="dv")
            prev["sle"] = msl
        moev = p.add("scalar", lambda e, co=co, qch=qch, qp=qp: e.activation(
            ar(c, M2_XNO + qch * 1024 + co, 256)[qp:qp + 64, :], c.pf[2][0:64, 0:256],
            AF.Identity), waits=[momm], inc="ac")
        prev["oev"] = moev
    mste = p.add("vector", lambda e, sidx=sidx: e.tensor_copy(
        arf(c, M2_ST, 2048)[:, sidx * 64:(sidx + 1) * 64], c.pf[5][:, 0:64]),
        waits=[prev["smm"]], inc="dv")
    prev["ste"] = mste
    p.flush()




M1_XN0, M1_XN1, M1_C1B, M1_C2B, M1_FOUT = 0, 8192, 16384, 24576, 32768


def emit_l1_ffn(p, c):
    emit_pre_ln(p, c, 0, lambda ch: ar(c, M1_XN0 + ch * 1024, 1024))
    emit_pre_ln(p, c, 1, lambda ch: ar(c, M1_XN1 + ch * 1024, 1024))
    ms = p.add("sync", lambda e: e.dma_start(out=c.d_xsp2.ap(), in_=c.x_res[:]), inc="sx", k=16)
    p.add("sync", lambda e: e.nop(), waits=[ms])
    p.flush()

    prev = {}
    for gi, un in enumerate(("uf", "ui", "uc")):
        uap = c.d_l1f_w[un]
        for mo in range(KC):
            buf = (gi * KC + mo) % 2
            wmk = p.add("gpsimd", lambda e, uap=uap, mo=mo, buf=buf: e.dma_start(
                out=c.aw[:, buf * 1024:(buf + 1) * 1024].rearrange("p (kc n) -> p kc n", kc=KC),
                in_=uap.rearrange("(kc p) m -> p kc m", p=128)[:, :, mo * 128:(mo + 1) * 128]),
                waits=[prev.get(("b0", gi * KC + mo - 2))], inc="gw", k=16)

            def b0mm(e, buf=buf, gi=gi):
                for kc in range(KC):
                    mm = e.matmul(c.pf[gi][:, 0:1],
                                  c.aw[:, buf * 1024 + kc * 128: buf * 1024 + (kc + 1) * 128],
                                  c.in_c_b[:, kc:kc + 1], start=(kc == 0), stop=(kc == KC - 1))
                return mm
            mmk = p.add("tensor", b0mm, waits=[wmk, prev.get(("b0e", gi * KC + mo - 1))], inc="pe")
            prev[("b0", gi * KC + mo)] = mmk
            mev = p.add("vector", lambda e, gi=gi, mo=mo: e.tensor_add(
                c.bias0[:, gi * 8 + mo: gi * 8 + mo + 1], c.pf[gi][:, 0:1],
                c.l1f_bias[:, gi * 8 + mo: gi * 8 + mo + 1]), waits=[mmk], inc="dv")
            prev[("b0e", gi * KC + mo)] = mev
    p.flush()

    WF = {"f": ("wf", None, 0), "i": ("wi", None, 1), "c": ("wc", None, 2)}
    _l1f_step(p, c, step=0, xn_off=M1_XN0, cprev_off=None, cout_b=0, cbf_off=M1_C1B)
    _l1f_step(p, c, step=1, xn_off=M1_XN1, cprev_off=M1_C1B, cout_b=1, cbf_off=M1_C2B)

    for t in (0, 1):
        xn_off = M1_XN0 if t == 0 else M1_XN1
        prev = {}
        for mo in range(KC):
            buf = mo % 2
            w1m = p.add("gpsimd", lambda e, mo=mo, buf=buf: e.dma_start(
                out=c.aw[:, buf * 1024:(buf + 1) * 1024].rearrange("p (kc n) -> p kc n", kc=KC),
                in_=c.d_l1f_w["wo2"].rearrange("(kc p) m -> p kc m", p=128)[:, :, mo * 128:(mo + 1) * 128]),
                waits=[prev.get(("mm", mo - 2))], inc="gw", k=16)
            w2m = p.add("gpsimd", lambda e, mo=mo, buf=buf: e.dma_start(
                out=c.aw[:, 2048 + buf * 1024:2048 + (buf + 1) * 1024].rearrange("p (kc n) -> p kc n", kc=KC),
                in_=c.d_l1f_w["uo"].rearrange("(kc p) m -> p kc m", p=128)[:, :, mo * 128:(mo + 1) * 128]),
                waits=[prev.get(("mm", mo - 2))], inc="gw", k=16)
            for tt in range(2):
                idx = mo * 2 + tt
                pbank = c.pf[4 + idx % 2]

                def hmm(e, mo=mo, tt=tt, buf=buf, pbank=pbank, xn_off=xn_off):
                    for kc in range(KC):
                        e.matmul(pbank[:, :], c.aw[:, buf * 1024 + kc * 128: buf * 1024 + (kc + 1) * 128],
                                 ar(c, xn_off + kc * 1024 + tt * 512, 512),
                                 start=(kc == 0), stop=False)
                    for kc in range(KC):
                        mm = e.matmul(pbank[:, :],
                                      c.aw[:, 2048 + buf * 1024 + kc * 128: 2048 + buf * 1024 + (kc + 1) * 128],
                                      ar(c, M1_C2B + kc * 1024 + tt * 512, 512),
                                      start=False, stop=(kc == KC - 1))
                    return mm
                mmk = p.add("tensor", hmm, waits=[w1m, w2m, prev.get(("ev", idx - 2))], inc="pe")
                if tt == 1:
                    prev[("mm", mo)] = mmk
                mev = p.add("scalar", lambda e, mo=mo, tt=tt, pbank=pbank: e.activation(
                    ar(c, M1_FOUT + mo * 1024 + tt * 512, 512), pbank[:, :], AF.Tanh,
                    bias=c.l1f_bias[:, 24 + mo:24 + mo + 1]), waits=[mmk], inc="ac")
                prev[("ev", idx)] = mev
        p.flush()
        mr1 = p.add("sync", lambda e, t=t: e.dma_start(
            out=arf(c, M1_XN0, 4096).rearrange("p (ch u) -> p ch u", ch=4),
            in_=c.d_xsp2.ap().rearrange("p (ch two u) -> p ch two u", ch=KC, two=2)[:, 0:4, t, :]),
            inc="sx", k=16)
        mr2 = p.add("sync", lambda e, t=t: e.dma_start(
            out=arf(c, M1_C1B, 4096).rearrange("p (ch u) -> p ch u", ch=4),
            in_=c.d_xsp2.ap().rearrange("p (ch two u) -> p ch two u", ch=KC, two=2)[:, 4:8, t, :]),
            inc="sx", k=16)
        p.add("sync", lambda e: e.nop(), waits=[mr1, mr2])
        p.flush()
        emit_post_ln_residual(
            p, c, t, lambda ch: ar(c, M1_FOUT + ch * 1024, 1024), src_is_bf=True,
            x_old_fn=lambda ch: (arf(c, M1_XN0 + ch * 2048, 1024) if ch < 4
                                 else arf(c, M1_C1B + (ch - 4) * 2048, 1024)))


def _l1f_step(p, c, step, xn_off, cprev_off, cout_b, cbf_off):
    names = (("wf", "uf"), ("wi", "ui"), ("wc", "uc"))
    prev = {}
    for mo in range(KC):
        buf = mo % 2
        wmks = []
        for gi, (wn, un) in enumerate(names):
            base = gi * 2048
            wmks.append(p.add("gpsimd", lambda e, wn=wn, mo=mo, buf=buf, base=base: e.dma_start(
                out=c.aw[:, base + buf * 1024: base + (buf + 1) * 1024].rearrange("p (kc n) -> p kc n", kc=KC),
                in_=c.d_l1f_w[wn].rearrange("(kc p) m -> p kc m", p=128)[:, :, mo * 128:(mo + 1) * 128]),
                waits=[prev.get(("mm", gi, mo - 2))], inc="gw", k=16))
            if step == 1:
                wmks.append(p.add("gpsimd", lambda e, un=un, mo=mo, buf=buf, base=base: e.dma_start(
                    out=c.aw[:, 6144 + base + buf * 1024: 6144 + base + (buf + 1) * 1024].rearrange(
                        "p (kc n) -> p kc n", kc=KC),
                    in_=c.d_l1f_w[un].rearrange("(kc p) m -> p kc m", p=128)[:, :, mo * 128:(mo + 1) * 128]),
                    waits=[prev.get(("mm", gi, mo - 2))], inc="gw", k=16))
        for tt in range(2):
            evs = []
            for gi in range(3):
                base = gi * 2048
                pbank = c.pf[gi + 3 * (tt % 2)]

                def gmm(e, gi=gi, mo=mo, tt=tt, buf=buf, base=base, pbank=pbank):
                    for kc in range(KC):
                        mm = e.matmul(pbank[:, :],
                                      c.aw[:, base + buf * 1024 + kc * 128: base + buf * 1024 + (kc + 1) * 128],
                                      ar(c, xn_off + kc * 1024 + tt * 512, 512),
                                      start=(kc == 0), stop=(step == 0 and kc == KC - 1))
                    if step == 1:
                        for kc in range(KC):
                            mm = e.matmul(pbank[:, :],
                                          c.aw[:, 6144 + base + buf * 1024 + kc * 128: 6144 + base + buf * 1024 + (kc + 1) * 128],
                                          ar(c, M1_C1B + kc * 1024 + tt * 512, 512),
                                          start=False, stop=(kc == KC - 1))
                    return mm
                mmk = p.add("tensor", gmm,
                            waits=wmks + [prev.get(("gev", gi, mo * 2 + tt - 2))], inc="pe")
                prev[("mm", gi, mo)] = mmk if tt == 1 else prev.get(("mm", gi, mo))
                func = AF.Tanh if gi == 2 else AF.Sigmoid
                if step == 0:
                    bcol = c.bias0[:, gi * 8 + mo: gi * 8 + mo + 1]
                else:
                    bcol = c.l1f_bias[:, gi * 8 + mo: gi * 8 + mo + 1]
                dst = (c.lntmp[:, tt * 512:(tt + 1) * 512] if gi == 0 else
                       c.lntmp[:, 1024 + tt * 512:1024 + (tt + 1) * 512] if gi == 1 else
                       c.lnsq[:, tt * 512:(tt + 1) * 512])
                mev = p.add("scalar", lambda e, dst=dst, pbank=pbank, func=func, bcol=bcol: e.activation(
                    dst, pbank[:, :], func, bias=bcol),
                    waits=[mmk, prev.get(("cu", mo * 2 + tt - 2))], inc="ac")
                prev[("gev", gi, mo * 2 + tt)] = mev
                evs.append(mev)

            def cupd(e, mo=mo, tt=tt):
                ft = c.lntmp[:, tt * 512:(tt + 1) * 512]
                it = c.lntmp[:, 1024 + tt * 512:1024 + (tt + 1) * 512]
                gt = c.lnsq[:, tt * 512:(tt + 1) * 512]
                tmp = c.lnsq[:, 1024 + tt * 512:1024 + (tt + 1) * 512]
                cdst = xv(c, mo, cout_b)[:, tt * 512:(tt + 1) * 512]
                e.tensor_mul(tmp, it, gt)
                if step == 0:
                    e.scalar_tensor_tensor(cdst, ft, c.in_c_f[:, mo:mo + 1], tmp,
                                           ALU.mult, ALU.add)
                else:
                    cprev = xv(c, mo, 0)[:, tt * 512:(tt + 1) * 512]
                    e.tensor_tensor(c.lnft[:, 0:512], ft, cprev, ALU.mult)
                    e.tensor_add(cdst, c.lnft[:, 0:512], tmp)
                return e.tensor_copy(ar(c, cbf_off + mo * 1024 + tt * 512, 512), cdst)
            mcu = p.add("vector", cupd, waits=evs, inc="dv")
            prev[("cu", mo * 2 + tt)] = mcu
    p.flush()

